# version 07
# speedup vs baseline: 1.2002x; 1.0173x over previous
import numpy as np
import ml_dtypes
from contextlib import ExitStack
import concourse.bass as bass
import concourse.mybir as mybir
from concourse.bass_utils import run_bass_kernel_spmd

F32 = mybir.dt.float32
BF16 = mybir.dt.bfloat16
I32 = mybir.dt.int32
AF = mybir.ActivationFunctionType
ALU = mybir.AluOpType
AX = mybir.AxisListType

B, S, D = 2, 8192, 2048
TOK = 2048
HPC = 4
SCL = float(96.0 ** -0.25)
EPS_LN = 1e-5
EPS_RMS = 1e-6
GRP4 = [[0, 1, 2, 3], [4, 5, 6, 7]]
GRP8 = [[0, 1, 2, 3, 4, 5, 6, 7]]


class Sched:

    def __init__(self, nc, sem_names):
        self.nc = nc
        self.events = []
        self.cnt = {s: 0 for s in sem_names}
        self.sems = {s: nc.semaphore(s).__enter__() for s in sem_names}

    def mark(self, s):
        return self.cnt[s]

    def add(self, engine, waits, emit, incs=()):
        for s, k in incs:
            self.cnt[s] += k
        sems = self.sems

        def closure(eng, waits=tuple(waits), incs=tuple(incs), emit=emit):
            for s, v in waits:
                if v > 0:
                    eng.wait_ge(sems[s], v)
            ins = emit(eng)
            for s, k in incs:
                ins.then_inc(sems[s], k)
        self.events.append((engine, closure))

    def replay(self, blk):
        evs, self.events = self.events, []
        for name, method in (("sync", blk.sync), ("gpsimd", blk.gpsimd),
                             ("tensor", blk.tensor), ("vector", blk.vector),
                             ("scalar", blk.scalar)):
            mine = [c for (e, c) in evs if e == name]
            if not mine:
                continue

            def body(eng, mine=mine):
                for c in mine:
                    c(eng)
            method(body)


def build(n_phases=6, debug=False, trunc=0, down_hbs=4):
    nc = bass.Bass("TRN2", target_bir_lowering=False)
    dbg = "ExternalOutput" if debug else "Internal"

    x_in = nc.dram_tensor("x_in", [TOK, D], F32, kind="ExternalInput").ap()
    wg_raw = nc.dram_tensor("wg_raw", [1024, D], I32, kind="ExternalInput").ap()
    wu_raw = nc.dram_tensor("wu_raw", [1024, D], I32, kind="ExternalInput").ap()
    wd_raw = nc.dram_tensor("wd_raw", [D, 1024], I32, kind="ExternalInput").ap()
    gscT = nc.dram_tensor("gscT", [128, 8], F32, kind="ExternalInput").ap()
    uscT = nc.dram_tensor("uscT", [128, 8], F32, kind="ExternalInput").ap()
    dscT = nc.dram_tensor("dscT", [128, 16], F32, kind="ExternalInput").ap()
    wcq = nc.dram_tensor("wcq", [D, 768], F32, kind="ExternalInput").ap()
    wckv = nc.dram_tensor("wckv", [D, 256], F32, kind="ExternalInput").ap()
    wkr = nc.dram_tensor("wkr", [D, 32], F32, kind="ExternalInput").ap()
    wproj = nc.dram_tensor("wproj", [256, D], F32, kind="ExternalInput").ap()
    wqn = nc.dram_tensor("wqn", [768, 256], F32, kind="ExternalInput").ap()
    wqr = nc.dram_tensor("wqr", [768, 128], F32, kind="ExternalInput").ap()
    wkn = nc.dram_tensor("wkn", [256, 256], F32, kind="ExternalInput").ap()
    wv = nc.dram_tensor("wv", [256, 256], F32, kind="ExternalInput").ap()
    qgT = nc.dram_tensor("qgT", [128, 6], F32, kind="ExternalInput").ap()
    kvgT = nc.dram_tensor("kvgT", [128, 2], F32, kind="ExternalInput").ap()
    cosq = nc.dram_tensor("cosq", [32, S], F32, kind="ExternalInput").ap()
    cosk = nc.dram_tensor("cosk", [32, TOK], F32, kind="ExternalInput").ap()
    idnf = nc.dram_tensor("idnf", [128, 128], F32, kind="ExternalInput").ap()
    idnb = nc.dram_tensor("idnb", [128, 128], BF16, kind="ExternalInput").ap()
    masks = nc.dram_tensor("masks", [128, 4 * 512], BF16, kind="ExternalInput").ap()
    y_out = nc.dram_tensor("y_out", [TOK, D], F32, kind="ExternalOutput").ap()

    x1d = nc.dram_tensor("x1d", [TOK, D], F32, kind=dbg).ap()
    x2d = nc.dram_tensor("x2d", [TOK, D], F32, kind=dbg).ap()
    wgt_in = nc.dram_tensor("wgt_in", [D, 1024], BF16, kind="Internal").ap()
    wut_in = nc.dram_tensor("wut_in", [D, 1024], BF16, kind="Internal").ap()
    wdt_in = nc.dram_tensor("wdt_in", [1024, D], BF16, kind="Internal").ap()
    wgt_all = nc.dram_tensor("wgt_all", [8 * D, 1024], BF16, kind="Internal", addr_space="Shared").ap()
    wut_all = nc.dram_tensor("wut_all", [8 * D, 1024], BF16, kind="Internal", addr_space="Shared").ap()
    wdt_all = nc.dram_tensor("wdt_all", [8 * 1024, D], BF16, kind="Internal", addr_space="Shared").ap()
    nq_in = nc.dram_tensor("nq_in", [768, TOK], BF16, kind="Internal").ap()
    nkv_in = nc.dram_tensor("nkv_in", [256, TOK], BF16, kind="Internal").ap()
    krp_in = nc.dram_tensor("krp_in", [32, TOK], BF16, kind="Internal").ap()
    nq_all = nc.dram_tensor("nq_all", [4 * 768, TOK], BF16, kind="Internal").ap()
    nkv_all = nc.dram_tensor("nkv_all", [4 * 256, TOK], BF16, kind="Internal").ap()
    krp_all = nc.dram_tensor("krp_all", [4 * 32, TOK], BF16, kind="Internal").ap()
    o_in = nc.dram_tensor("o_in", [256, S], BF16, kind=dbg).ap()
    wdump = nc.dram_tensor("wdump", [8 * D, 1024], BF16, kind=dbg).ap()
    if debug:
        nq_dbg = nc.dram_tensor("nq_dbg", [4 * 768, TOK], BF16, kind="ExternalOutput").ap()
        nkv_dbg = nc.dram_tensor("nkv_dbg", [4 * 256, TOK], BF16, kind="ExternalOutput").ap()
        krp_dbg = nc.dram_tensor("krp_dbg", [4 * 32, TOK], BF16, kind="ExternalOutput").ap()
    yp_in_c = [nc.dram_tensor(f"yp_in{c}", [2048, D], BF16, kind="Internal").ap()
               for c in range(4)]
    yp_out_c = [nc.dram_tensor(f"yp_out{c}", [512, D], BF16, kind="Internal").ap()
                for c in range(4)]

    sems = ["sx", "sx0", "sx1", "sln", "str", "ste", "swg", "swg0", "swg1",
            "sgu", "spg", "spu", "ssi", "shm", "swd0", "swd1", "sxo",
            "sxr0", "sxr1", "so0", "so1", "sy0", "sy1", "scc",
            "sw0", "sw1", "swp", "skr0", "skr1", "smk", "snq", "sv0", "st0"]

    es = ExitStack()
    ten = es.enter_context
    idn_f = ten(nc.sbuf_tensor("idn_f", [128, 128], F32))
    idn_b = ten(nc.sbuf_tensor("idn_b", [128, 128], BF16))
    ones_b = ten(nc.sbuf_tensor("ones_b", [128, 1], BF16))
    cst = ten(nc.sbuf_tensor("cst", [128, 4], F32))
    wqn_sb = ten(nc.sbuf_tensor("wqn_sb", [128, 6 * 256], BF16))
    wqr_sb = ten(nc.sbuf_tensor("wqr_sb", [128, 6 * 128], BF16))
    wkn_sb = ten(nc.sbuf_tensor("wkn_sb", [128, 2 * 256], BF16))
    wv_sb = ten(nc.sbuf_tensor("wv_sb", [128, 2 * 256], BF16))

    S_ = Sched(nc, sems)
    A = S_.add
    M = S_.mark

    with ExitStack() as _es:
        en_ = _es.enter_context
        wrow0 = en_(nc.sbuf_tensor("wrow0", [128, 2048], BF16))
        wrow1 = en_(nc.sbuf_tensor("wrow1", [128, 2048], BF16))
        wdq0 = en_(nc.sbuf_tensor("wdq0", [128, 2048], BF16))
        wdq1 = en_(nc.sbuf_tensor("wdq1", [128, 2048], BF16))
        scl_sb = en_(nc.sbuf_tensor("scl_sb", [128, 32], F32))
        m128_sb = en_(nc.sbuf_tensor("m128_sb", [128, 32], F32))
        wstg = en_(nc.sbuf_tensor("wstg", [128, 16 * 1024], BF16))
        psT0 = en_(nc.psum_tensor("psT0", [128, 1024], BF16))
        psT1 = en_(nc.psum_tensor("psT1", [128, 1024], BF16))
        blk0 = en_(nc.Block())
        wrow, wdq, psT = [wrow0, wrow1], [wdq0, wdq1], [psT0, psT1]

        A("sync", [], lambda e: e.dma_start(idn_f[:], idnf), [("sx", 16)])
        A("sync", [], lambda e: e.dma_start(idn_b[:], idnb), [("sx", 16)])
        A("sync", [], lambda e: e.dma_start(scl_sb[:, 0:8], gscT), [("sx", 16)])
        A("sync", [], lambda e: e.dma_start(scl_sb[:, 8:16], uscT), [("sx", 16)])
        A("sync", [], lambda e: e.dma_start(scl_sb[:, 16:32], dscT), [("sx", 16)])
        mconst = M("sx")
        A("vector", [("sx", mconst)], lambda e: e.memset(ones_b[:], 1.0), [("sv0", 1)])
        A("vector", [], lambda e: e.memset(cst[:, 0:1], EPS_LN), [("sv0", 1)])
        A("vector", [], lambda e: e.memset(cst[:, 1:2], EPS_RMS), [("sv0", 1)])
        A("vector", [], lambda e: e.memset(cst[:, 2:3], 1e-30), [("sv0", 1)])
        A("vector", [("sx", mconst)],
          lambda e: e.tensor_scalar_mul(m128_sb[:], scl_sb[:], -128.0), [("sv0", 1)])
        m_m128 = M("sv0")

        def conv_weight(raw, ntiles, scol, cc_dst, stage_cols, tpt, ncols):
            prev_ste = M("ste")
            prev_dma = M("swg")
            deq_m, tr_m, ev_m = [], [], []
            for t in range(ntiles):
                w = [("ssi", deq_m[t - 2])] if t >= 2 else []
                src = raw.rearrange("(t p) n -> t p n", p=128)[t]
                sld = "sw0" if t % 2 == 0 else "sw1"
                A("gpsimd", w,
                  lambda e, t=t, src=src: e.dma_start(wrow[t % 2][:, 0:ncols], src),
                  [(sld, 16)])
                ml = M(sld)
                w = [(sld, ml), ("sv0", m_m128)]
                if t >= 2:
                    w.append(("st0", tr_m[(t - 2) * tpt + tpt - 1]))
                A("vector", w,
                  lambda e, t=t: e.tensor_scalar(wdq[t % 2][:, 0:ncols], wrow[t % 2][:, 0:ncols],
                                                 scl_sb[:, scol + t:scol + t + 1],
                                                 m128_sb[:, scol + t:scol + t + 1],
                                                 ALU.mult, ALU.add),
                  [("ssi", 1)])
                deq_m.append(M("ssi"))
                for j in range(tpt):
                    jj = t * tpt + j
                    w = [("ssi", deq_m[t])]
                    if jj >= 2:
                        w.append(("ste", ev_m[jj - 2]))
                    else:
                        w.append(("ste", prev_ste))
                    A("tensor", w,
                      lambda e, t=t, j=j: e.transpose(psT[jj_par(t, j, tpt)][:, 0:128],
                                                      wdq[t % 2][:, j * 128:(j + 1) * 128], idn_b[:]),
                      [("st0", 1)])
                    tr_m.append(M("st0"))
                    A("vector", [("st0", tr_m[-1]), ("swg", prev_dma)],
                      lambda e, t=t, j=j: e.tensor_copy(
                          wstg[:, j * stage_cols + t * 128: j * stage_cols + t * 128 + 128],
                          psT[jj_par(t, j, tpt)][:, 0:128]),
                      [("ste", 1)])
                    ev_m.append(M("ste"))
            mv = M("ste")
            nout = ncols // 128
            for j in range(nout):
                A("sync", [("ste", mv)],
                  lambda e, j=j: e.dma_start(cc_dst.rearrange("(j p) n -> j p n", p=128)[j],
                                             wstg[:, j * stage_cols:(j + 1) * stage_cols]),
                  [("swg", 16)])
            return M("swg")

        def jj_par(t, j, tpt):
            return (t * tpt + j) % 2

        m1 = conv_weight(wg_raw, 8, 0, wgt_in, 1024, 16, 2048)
        A("gpsimd", [("swg", m1)],
          lambda e: e.collective_compute("AllGather", ALU.bypass, replica_groups=GRP8,
                                         ins=[wgt_in], outs=[wgt_all]), [("scc", 1)])
        m2 = conv_weight(wu_raw, 8, 8, wut_in, 1024, 16, 2048)
        A("gpsimd", [("swg", m2)],
          lambda e: e.collective_compute("AllGather", ALU.bypass, replica_groups=GRP8,
                                         ins=[wut_in], outs=[wut_all]), [("scc", 1)])
        mscc_gu = M("scc")
        m3 = conv_weight(wd_raw, 16, 16, wdt_in, 2048, 8, 1024)
        A("gpsimd", [("swg", m3)],
          lambda e: e.collective_compute("AllGather", ALU.bypass, replica_groups=GRP8,
                                         ins=[wdt_in], outs=[wdt_all]), [("scc", 1)])
        mscc_dd = M("scc")
        if debug and trunc == 9:
            for j in range(16):
                A("gpsimd", [("scc", M("scc"))],
                  lambda e, j=j: e.dma_start(wdump.rearrange("(j p) n -> j p n", p=1024)[j],
                                             wut_all.rearrange("(j p) n -> j p n", p=1024)[j]),
                  [("so0", 16)])
            A("gpsimd", [("so0", M("so0"))], lambda e: e.nop())
        S_.replay(blk0)

    if n_phases <= 0:
        return nc

    def emit_ln(src_ap, xt_buf, xsq, lnst, c0, af_buf, wait_load, wait_sq, wait_stats, wait_apply, sld="sx"):
        A("sync", wait_load, lambda e: e.dma_start(xt_buf[:], src_ap), [(sld, 16)])
        ml = M(sld)
        A("scalar", [(sld, ml)] + wait_sq,
          lambda e: e.activation(xsq[:], xt_buf[:], AF.Square), [("ssi", 1)])
        msq = M("ssi")
        A("vector", [(sld, ml)] + wait_stats,
          lambda e: e.reduce_sum(lnst[:, c0:c0 + 1], xt_buf[:], axis=AX.X), [("sv0", 1)])
        A("vector", [("ssi", msq), ("sv0", M("sv0"))],
          lambda e: e.reduce_sum(lnst[:, c0 + 1:c0 + 2], xsq[:], axis=AX.X), [("sv0", 1)])
        msum = M("sv0")
        A("vector", [("sv0", msum)],
          lambda e: e.tensor_scalar_mul(lnst[:, c0 + 2:c0 + 3], lnst[:, c0:c0 + 1], 1.0 / 2048.0),
          [("sv0", 1)])
        A("vector", [("sv0", M("sv0"))],
          lambda e: e.tensor_mul(lnst[:, c0 + 3:c0 + 4], lnst[:, c0 + 2:c0 + 3], lnst[:, c0 + 2:c0 + 3]),
          [("sv0", 1)])
        A("vector", [("sv0", M("sv0"))],
          lambda e: e.scalar_tensor_tensor(lnst[:, c0 + 4:c0 + 5], lnst[:, c0 + 1:c0 + 2], 1.0 / 2048.0,
                                           lnst[:, c0 + 3:c0 + 4], ALU.mult, ALU.subtract), [("sv0", 1)])
        A("scalar", [("sv0", M("sv0"))],
          lambda e: e.activation(lnst[:, c0 + 5:c0 + 6], lnst[:, c0 + 4:c0 + 5], AF.Sqrt,
                                 bias=cst[:, 0:1], scale=1.0), [("ssi", 1)])
        A("vector", [("ssi", M("ssi"))],
          lambda e: e.reciprocal(lnst[:, c0 + 6:c0 + 7], lnst[:, c0 + 5:c0 + 6]), [("sv0", 1)])
        A("vector", [("sv0", M("sv0"))],
          lambda e: e.tensor_scalar(lnst[:, c0 + 7:c0 + 8], lnst[:, c0 + 2:c0 + 3],
                                    lnst[:, c0 + 6:c0 + 7], -1.0, ALU.mult, ALU.mult), [("sv0", 1)])
        A("scalar", [("sv0", M("sv0"))] + wait_apply,
          lambda e: e.activation(af_buf[:], xt_buf[:], AF.Identity, bias=lnst[:, c0 + 7:c0 + 8],
                                 scale=lnst[:, c0 + 6:c0 + 7]), [("sln", 1)])
        return M("sln"), msum

    def ffn_pass(src_ln, src_res, dst, out_sems, pfx, wgu_scc=0, wdd_scc=0):
        with ExitStack() as _es:
            en_ = _es.enter_context
            xt0 = en_(nc.sbuf_tensor(pfx + "xt0", [128, 2048], F32))
            xt1 = en_(nc.sbuf_tensor(pfx + "xt1", [128, 2048], F32))
            xsq = en_(nc.sbuf_tensor(pfx + "xsq", [128, 2048], F32))
            lnst = en_(nc.sbuf_tensor(pfx + "lnst", [128, 16], F32))
            af0 = en_(nc.sbuf_tensor(pfx + "af0", [128, 2048], F32))
            af1 = en_(nc.sbuf_tensor(pfx + "af1", [128, 2048], F32))
            aT = en_(nc.sbuf_tensor(pfx + "aT", [128, 16 * 512], BF16))
            hT = en_(nc.sbuf_tensor(pfx + "hT", [128, 64 * 512], BF16))
            wgu0 = en_(nc.sbuf_tensor(pfx + "wgu0", [128, 2 * 16 * 256], BF16))
            wgu1 = en_(nc.sbuf_tensor(pfx + "wgu1", [128, 2 * 16 * 256], BF16))
            wdd0 = en_(nc.sbuf_tensor(pfx + "wdd0", [128, 8 * 512], BF16))
            wdd1 = en_(nc.sbuf_tensor(pfx + "wdd1", [128, 8 * 512], BF16))
            sg0 = en_(nc.sbuf_tensor(pfx + "sg0", [128, 512], F32))
            sg1 = en_(nc.sbuf_tensor(pfx + "sg1", [128, 512], F32))
            xr0 = en_(nc.sbuf_tensor(pfx + "xr0", [128, 4 * 512], F32))
            xr1 = en_(nc.sbuf_tensor(pfx + "xr1", [128, 4 * 512], F32))
            xo0 = en_(nc.sbuf_tensor(pfx + "xo0", [128, 512], F32))
            xo1 = en_(nc.sbuf_tensor(pfx + "xo1", [128, 512], F32))
            pg0 = en_(nc.psum_tensor(pfx + "pg0", [128, 512], F32))
            pg1 = en_(nc.psum_tensor(pfx + "pg1", [128, 512], F32))
            pu0 = en_(nc.psum_tensor(pfx + "pu0", [128, 512], F32))
            pu1 = en_(nc.psum_tensor(pfx + "pu1", [128, 512], F32))
            pd0 = en_(nc.psum_tensor(pfx + "pd0", [128, 512], F32))
            pd1 = en_(nc.psum_tensor(pfx + "pd1", [128, 512], F32))
            pd2 = en_(nc.psum_tensor(pfx + "pd2", [128, 512], F32))
            pd3 = en_(nc.psum_tensor(pfx + "pd3", [128, 512], F32))
            blkF = en_(nc.Block())
            xt, af = [xt0, xt1], [af0, af1]
            wgu, wdd = [wgu0, wgu1], [wdd0, wdd1]
            sg, xr, xo = [sg0, sg1], [xr0, xr1], [xo0, xo1]
            pg, pu, pd = [pg0, pg1], [pu0, pu1], [pd0, pd1, pd2, pd3]

            ln_m = []
            ln_sum = []
            tr_tile = []
            ev_done = []
            silu_m = []
            hm_m = []
            gu_cons = {}
            dc_cons = {}
            dp_m = {}
            xo_m = {}
            out_m = {}
            tr_ev = []
            silu_blk_end = []

            for tb in range(4):
                for tt in range(4):
                    n = tb * 4 + tt
                    wl = [("sln", ln_m[n - 2])] if n >= 2 else []
                    wsq = [("sv0", ln_sum[n - 1])] if n >= 1 else []
                    wst = [("sln", ln_m[n - 2])] if n >= 2 else []
                    wap = [("str", tr_tile[n - 2])] if n >= 2 else []
                    src = src_ln.rearrange("(n p) d -> n p d", p=128)[n]
                    am, sm = emit_ln(src, xt[n % 2], xsq, lnst, 8 * (n % 2), af[n % 2],
                                     wl, wsq, wst, wap, sld="sx0" if n % 2 == 0 else "sx1")
                    ln_m.append(am)
                    ln_sum.append(sm)
                    for j in range(16):
                        jj = n * 16 + j
                        w = [("sln", ln_m[n])]
                        if jj >= 2:
                            w.append(("ste", tr_ev[jj - 2]))
                        if tb > 0 and (jj % 64) < 2 and trunc not in (1,):
                            w.append(("ssi", silu_blk_end[tb - 1]))
                            w.append(("spu", gu_cons[(tb - 1, 31)]))
                        A("tensor", w,
                          lambda e, n=n, j=j: e.transpose(pg[j % 2][:, 0:128],
                                                          af[n % 2][:, j * 128:(j + 1) * 128], idn_f[:]),
                          [("str", 1)])
                        mt = M("str")
                        A("vector", [("str", mt)],
                          lambda e, j=j, tt=tt: e.tensor_copy(
                              aT[:, j * 512 + tt * 128: j * 512 + tt * 128 + 128], pg[j % 2][:, 0:128]),
                          [("ste", 1)])
                        tr_ev.append(M("ste"))
                    tr_tile.append(M("str"))
                    if trunc == 1:
                        dstv = dst.rearrange("(n p) d -> n p d", p=128)[n]
                        A("gpsimd", [("sln", ln_m[n])],
                          lambda e, n=n, dstv=dstv: e.dma_start(dstv, af[n % 2][:]),
                          [(out_sems[n % 2], 16)])
                ev_done.append(M("ste"))
                if trunc == 1:
                    silu_blk_end.append(0)
                    continue

                for ib in range(32):
                    r, ilo = ib // 4, (ib % 4) * 256
                    w = [("scc", wgu_scc)] if wgu_scc else []
                    if (tb, ib - 2) in gu_cons:
                        w.append(("spu", gu_cons[(tb, ib - 2)]))
                    elif tb > 0:
                        w.append(("spu", gu_cons[(tb - 1, 30 + ib)]))
                    sld = "swg0" if ib % 2 == 0 else "swg1"
                    for half, wall in ((0, wgt_all), (1, wut_all)):
                        src = wall.rearrange("(r j p) n -> r p j n", j=16, p=128)[r][:, :, ilo:ilo + 256]
                        off = half * 16 * 256
                        A("sync", w if half == 0 else [],
                          lambda e, ib=ib, off=off, src=src: e.dma_start(
                              wgu[ib % 2][:, off:off + 16 * 256].rearrange("p (j n) -> p j n", j=16), src),
                          [(sld, 16)])
                    mw = M(sld)
                    for it in range(2):
                        m = ib * 2 + it
                        mg = tb * 64 + m
                        w = [(sld, mw), ("ste", ev_done[tb])]
                        if mg >= 2:
                            w.append(("ssi", silu_m[mg - 2]))
                            if trunc in (6, 7, 8):
                                w.append(("shm", hm_m[mg - 2]))

                        def gmms(e, ib=ib, it=it):
                            for j in range(16):
                                ins = e.matmul(pg[it % 2][:],
                                               wgu[ib % 2][:, j * 256 + it * 128: j * 256 + it * 128 + 128],
                                               aT[:, j * 512:(j + 1) * 512], start=(j == 0), stop=(j == 15))
                            return ins
                        A("tensor", w, gmms, [("spg", 1)])
                        mpg = M("spg")
                        w = [("shm", hm_m[mg - 2])] if mg >= 2 else []

                        def umms(e, ib=ib, it=it):
                            for j in range(16):
                                ins = e.matmul(pu[it % 2][:],
                                               wgu[ib % 2][:, (16 + j) * 256 + it * 128: (16 + j) * 256 + it * 128 + 128],
                                               aT[:, j * 512:(j + 1) * 512], start=(j == 0), stop=(j == 15))
                            return ins
                        A("tensor", w, umms, [("spu", 1)])
                        gu_cons[(tb, ib)] = M("spu")
                        mpu = M("spu")
                        wsl = [("spg", mpg)]
                        if mg >= 2:
                            wsl.append(("shm", hm_m[mg - 2]))
                        A("scalar", wsl,
                          lambda e, it=it, m=m: e.activation(sg[m % 2][:], pg[it % 2][:], AF.Silu),
                          [("ssi", 1)])
                        silu_m.append(M("ssi"))
                        w = [("ssi", silu_m[-1]), ("spu", mpu)]
                        if tb > 0 and m == 0 and trunc != 2:
                            w.append(("st0", dp_m[(tb - 1, down_hbs - 1, 3)]))
                        elif tb > 0 and m == 0 and trunc == 2:
                            w.append(("shm", hm_m[-1]))
                        A("vector", w,
                          (lambda e, it=it, m=m: e.tensor_copy(hT[:, m * 512:(m + 1) * 512],
                                                               pg[it % 2][:])) if trunc == 6 else
                          (lambda e, it=it, m=m: e.tensor_copy(hT[:, m * 512:(m + 1) * 512],
                                                               pu[it % 2][:])) if trunc == 7 else
                          (lambda e, it=it, m=m: e.tensor_copy(hT[:, m * 512:(m + 1) * 512],
                                                               sg[m % 2][:])) if trunc == 8 else
                          (lambda e, it=it, m=m: e.tensor_mul(hT[:, m * 512:(m + 1) * 512],
                                                              sg[m % 2][:], pu[it % 2][:])),
                          [("shm", 1)])
                        hm_m.append(M("shm"))
                silu_blk_end.append(silu_m[-1])
                if trunc in (5, 6, 7, 8) and tb == 0:
                    A("gpsimd", [("shm", M("shm"))],
                      lambda e: e.dma_start(dst.rearrange("(a p) d -> p a d", p=128),
                                            hT[:].rearrange("p (a d) -> p a d", a=16)),
                      [(out_sems[0], 16)])
                    break
                if trunc == 2:
                    for tt in range(4):
                        n = tb * 4 + tt
                        dstv = dst.rearrange("(n p) d -> n p d", p=128)[n]
                        A("gpsimd", [("shm", M("shm"))],
                          lambda e, n=n, dstv=dstv: e.dma_start(dstv, af[n % 2][:]),
                          [(out_sems[n % 2], 16)])
                    continue
                mhall = M("shm")
                for hb in range(down_hbs):
                    if trunc not in (3, 4):
                        w = []
                        if hb >= 2:
                            w += [("sxo", xo_m[(tb, hb - 2, tt)]) for tt in range(4)]
                        src = src_res.rearrange("(n p) d -> p n d", p=128)[:, 4 * tb:4 * tb + 4,
                                                                          512 * hb:512 * hb + 512]
                        sxr = "sxr0" if hb % 2 == 0 else "sxr1"
                        A("scalar", w,
                          lambda e, hb=hb, src=src: e.dma_start(
                              xr[hb % 2][:].rearrange("p (n d) -> p n d", n=4), src),
                          [(sxr, 16)])
                        mxr = M(sxr)
                    else:
                        sxr = "sxr0"
                    for ic in range(8):
                        w = [("scc", wdd_scc)] if wdd_scc else []
                        key = None
                        for cand in ((tb, hb, ic - 2), (tb, hb - 1, ic + 6), (tb - 1, 3, ic + 6)):
                            if cand in dc_cons:
                                key = cand
                                break
                        if key is not None:
                            w.append(("st0", dc_cons[key]))
                        src2 = wdt_all.rearrange("(ic ip p) n -> ic p ip n", ip=8, p=128)[ic][:, :, 512 * hb:512 * hb + 512]
                        sldd = "swd0" if ic % 2 == 0 else "swd1"
                        A("sync", w,
                          lambda e, ic=ic, src2=src2: e.dma_start(
                              wdd[ic % 2][:].rearrange("p (ip n) -> p ip n", ip=8), src2),
                          [(sldd, 16)])
                        mwd = M(sldd)
                        for tt in range(4):
                            w = [(sldd, mwd)]
                            if ic == 0:
                                w.append(("shm", mhall))
                                if hb > 0:
                                    w.append(("sxo", xo_m[(tb, hb - 1, tt)]))

                            def dmms(e, ic=ic, tt=tt):
                                for ip in range(8):
                                    it_g = 8 * ic + ip
                                    ins = e.matmul(pd[tt][:],
                                                   hT[:, it_g * 512 + tt * 128: it_g * 512 + tt * 128 + 128],
                                                   wdd[ic % 2][:, ip * 512:(ip + 1) * 512],
                                                   start=(ic == 0 and ip == 0), stop=(ic == 7 and ip == 7))
                                return ins
                            A("tensor", w, dmms, [("st0", 1)])
                            if ic == 7:
                                dp_m[(tb, hb, tt)] = M("st0")
                        dc_cons[(tb, hb, ic)] = M("st0")
                    for tt in range(4):
                        n = tb * 4 + tt
                        nlin = tb * 16 + hb * 4 + tt
                        w = [("st0", dp_m[(tb, hb, tt)])]
                        if trunc not in (3, 4):
                            w.append((sxr, mxr))
                        if (nlin - 2) in out_m:
                            w.append((out_sems[nlin % 2], out_m[nlin - 2]))
                        if trunc in (3, 4):
                            A("vector", w,
                              lambda e, tt=tt, nlin=nlin: e.tensor_copy(xo[nlin % 2][:], pd[tt][:]),
                              [("sxo", 1)])
                        else:
                            A("vector", w,
                              lambda e, tt=tt, hb=hb, nlin=nlin: e.scalar_tensor_tensor(
                                  xo[nlin % 2][:], pd[tt][:], 0.5, xr[hb % 2][:, tt * 512:(tt + 1) * 512],
                                  ALU.mult, ALU.add),
                              [("sxo", 1)])
                        xo_m[(tb, hb, tt)] = M("sxo")
                        dstv = dst.rearrange("(n p) d -> n p d", p=128)[n][:, 512 * hb:512 * hb + 512]
                        A("gpsimd", [("sxo", xo_m[(tb, hb, tt)])],
                          lambda e, nlin=nlin, dstv=dstv: e.dma_start(dstv, xo[nlin % 2][:]),
                          [(out_sems[nlin % 2], 16)])
                        out_m[nlin] = M(out_sems[nlin % 2])
            A("gpsimd", [(out_sems[0], M(out_sems[0])), (out_sems[1], M(out_sems[1]))],
              lambda e: e.nop())
            S_.replay(blkF)

    ffn_pass(x_in, x_in, x1d, ("so0", "so1"), "f1",
             wgu_scc=mscc_gu, wdd_scc=mscc_dd)
    if n_phases <= 1:
        return nc

    with ExitStack() as _es:
        en_ = _es.enter_context
        xt0b = en_(nc.sbuf_tensor("xt0b", [128, 2048], F32))
        xt1b = en_(nc.sbuf_tensor("xt1b", [128, 2048], F32))
        xsqb = en_(nc.sbuf_tensor("xsqb", [128, 2048], F32))
        lnstb = en_(nc.sbuf_tensor("lnstb", [128, 16], F32))
        afb0 = en_(nc.sbuf_tensor("afb0", [128, 2048], F32))
        afb1 = en_(nc.sbuf_tensor("afb1", [128, 2048], F32))
        a1T = en_(nc.sbuf_tensor("a1T", [128, 16 * 128], BF16))
        wcq_sb = en_(nc.sbuf_tensor("wcq_sb", [128, 16 * 768], BF16))
        wckv_sb = en_(nc.sbuf_tensor("wckv_sb", [128, 16 * 256], BF16))
        wkr_sb = en_(nc.sbuf_tensor("wkr_sb", [128, 16 * 32], BF16))
        wtmp = en_(nc.sbuf_tensor("wtmp", [128, 1536], F32))
        gg_sb = en_(nc.sbuf_tensor("gg_sb", [128, 8], F32))
        cosk_sb = en_(nc.sbuf_tensor("cosk_sb", [32, 2048], BF16))
        nq_sb = en_(nc.sbuf_tensor("nq_sb", [128, 768], BF16))
        nkv_sb = en_(nc.sbuf_tensor("nkv_sb", [128, 256], BF16))
        sqr_sb = en_(nc.sbuf_tensor("sqr_sb", [128, 1024], F32))
        rstat = en_(nc.sbuf_tensor("rstat", [128, 8], F32))
        nqT = en_(nc.sbuf_tensor("nqT", [128, 6 * 2048], BF16))
        nkvT = en_(nc.sbuf_tensor("nkvT", [128, 2 * 2048], BF16))
        krT = en_(nc.sbuf_tensor("krT", [32, 2048], BF16))
        pcq = en_(nc.psum_tensor("pcq", [128, 1024], F32))
        pkv = en_(nc.psum_tensor("pkv", [128, 512], F32))
        pkr = en_(nc.psum_tensor("pkr", [128, 512], F32))
        ptr0 = en_(nc.psum_tensor("ptr0", [128, 512], F32))
        ptr1 = en_(nc.psum_tensor("ptr1", [128, 512], F32))
        pnq0 = en_(nc.psum_tensor("pnq0", [128, 1024], BF16))
        pnq1 = en_(nc.psum_tensor("pnq1", [128, 1024], BF16))
        blk2 = en_(nc.Block())
        xtb, afb = [xt0b, xt1b], [afb0, afb1]
        ptr, pnq = [ptr0, ptr1], [pnq0, pnq1]

        A("gpsimd", [], lambda e: e.dma_start(
            wcq_sb[:].rearrange("p (j n) -> p j n", j=16),
            wcq.rearrange("(j p) n -> p j n", p=128)), [("sw0", 16)])
        A("gpsimd", [], lambda e: e.dma_start(
            wckv_sb[:].rearrange("p (j n) -> p j n", j=16),
            wckv.rearrange("(j p) n -> p j n", p=128)), [("sw0", 16)])
        A("gpsimd", [], lambda e: e.dma_start(cosk_sb[:], cosk), [("sw0", 16)])
        A("gpsimd", [], lambda e: e.dma_start(gg_sb[:, 0:6], qgT), [("sw0", 16)])
        A("gpsimd", [], lambda e: e.dma_start(gg_sb[:, 6:8], kvgT), [("sw0", 16)])
        mwcast = M("sw0")
        A("sync", [], lambda e: e.dma_start(
            wtmp[:, 0:512].rearrange("p (j n) -> p j n", j=16),
            wkr.rearrange("(j p) n -> p j n", p=128)), [("sx", 16)])
        mwkr = M("sx")
        wkr3 = wkr_sb[:].rearrange("p (j n) -> p j n", j=16)
        wt3 = wtmp[:, 0:512].rearrange("p (j n) -> p j n", j=16)
        A("vector", [("sx", mwkr)],
          lambda e: e.tensor_sub(wkr3[:, :, 0:32:2], wt3[:, :, 0:32:2], wt3[:, :, 1:32:2]), [("sv0", 1)])
        A("vector", [],
          lambda e: e.tensor_add(wkr3[:, :, 1:32:2], wt3[:, :, 0:32:2], wt3[:, :, 1:32:2]), [("sv0", 1)])
        mkrprep = M("sv0")
        A("sync", [("sv0", mkrprep)], lambda e: e.dma_start(
            wtmp[:].rearrange("p (j n) -> p j n", j=6),
            wqn.rearrange("(j p) n -> p j n", p=128)), [("sx", 16)])
        mq = M("sx")
        for j in range(6):
            A("vector", [("sx", mq), ("sw0", mwcast)],
              lambda e, j=j: e.tensor_scalar(wqn_sb[:, j * 256:(j + 1) * 256],
                                             wtmp[:, j * 256:(j + 1) * 256],
                                             gg_sb[:, j:j + 1], SCL, ALU.mult, ALU.mult),
              [("sv0", 1)])
        mf1 = M("sv0")
        A("sync", [("sv0", mf1)], lambda e: e.dma_start(
            wtmp[:, 0:768].rearrange("p (j n) -> p j n", j=6),
            wqr.rearrange("(j p) n -> p j n", p=128)), [("sx", 16)])
        mq2 = M("sx")
        wqr3s = wtmp[:, 0:768].rearrange("p (j n) -> p j n", j=6)
        wqr3d = wtmp[:, 768:1536].rearrange("p (j n) -> p j n", j=6)
        A("vector", [("sx", mq2)],
          lambda e: e.tensor_sub(wqr3d[:, :, 0:128:2], wqr3s[:, :, 0:128:2], wqr3s[:, :, 1:128:2]),
          [("sv0", 1)])
        A("vector", [],
          lambda e: e.tensor_add(wqr3d[:, :, 1:128:2], wqr3s[:, :, 0:128:2], wqr3s[:, :, 1:128:2]),
          [("sv0", 1)])
        mq3 = M("sv0")
        for j in range(6):
            A("vector", [("sv0", mq3)],
              lambda e, j=j: e.tensor_scalar_mul(wqr_sb[:, j * 128:(j + 1) * 128],
                                                 wtmp[:, 768 + j * 128: 768 + (j + 1) * 128],
                                                 gg_sb[:, j:j + 1]),
              [("sv0", 1)])
        mf2 = M("sv0")
        A("sync", [("sv0", mf2)], lambda e: e.dma_start(
            wtmp[:, 0:512].rearrange("p (j n) -> p j n", j=2),
            wkn.rearrange("(j p) n -> p j n", p=128)), [("sx", 16)])
        mk1 = M("sx")
        for j in range(2):
            A("vector", [("sx", mk1)],
              lambda e, j=j: e.tensor_scalar(wkn_sb[:, j * 256:(j + 1) * 256],
                                             wtmp[:, j * 256:(j + 1) * 256],
                                             gg_sb[:, 6 + j:7 + j], SCL, ALU.mult, ALU.mult),
              [("sv0", 1)])
        mf3 = M("sv0")
        A("sync", [("sv0", mf3)], lambda e: e.dma_start(
            wtmp[:, 0:512].rearrange("p (j n) -> p j n", j=2),
            wv.rearrange("(j p) n -> p j n", p=128)), [("sx", 16)])
        mk2 = M("sx")
        for j in range(2):
            A("vector", [("sx", mk2)],
              lambda e, j=j: e.tensor_scalar_mul(wv_sb[:, j * 256:(j + 1) * 256],
                                                 wtmp[:, j * 256:(j + 1) * 256],
                                                 gg_sb[:, 6 + j:7 + j]),
              [("sv0", 1)])
        mwprep = M("sv0")

        ln2_m = []
        ln2_sum = []
        tr2_tile = []
        nq_apply = []
        nkv_apply = []
        kr_done = []
        trq_done = []
        rms_red = []
        cq_done = []
        for n in range(16):
            wl = [("sln", ln2_m[n - 2])] if n >= 2 else []
            wsq = [("sv0", ln2_sum[n - 1])] if n >= 1 else []
            wst = [("sln", ln2_m[n - 2])] if n >= 2 else []
            wap = [("str", tr2_tile[n - 2])] if n >= 2 else []
            src = x1d.rearrange("(n p) d -> n p d", p=128)[n]
            am, sm = emit_ln(src, xtb[n % 2], xsqb, lnstb, 8 * (n % 2), afb[n % 2],
                             wl, wsq, wst, wap)
            ln2_m.append(am)
            ln2_sum.append(sm)
            trm = []
            for j in range(16):
                w = [("sln", ln2_m[n])]
                if j >= 2:
                    w.append(("ste", trm[j - 2]))
                if j < 2 and n > 0:
                    w.append(("st0", cq_done[n - 1]))
                A("tensor", w,
                  lambda e, n=n, j=j: e.transpose(ptr[j % 2][:, 0:128],
                                                  afb[n % 2][:, j * 128:(j + 1) * 128], idn_f[:]),
                  [("str", 1)])
                mt = M("str")
                A("vector", [("str", mt)],
                  lambda e, j=j: e.tensor_copy(a1T[:, j * 128:(j + 1) * 128], ptr[j % 2][:, 0:128]),
                  [("ste", 1)])
                trm.append(M("ste"))
            tr2_tile.append(M("str"))
            ma1 = M("ste")

            def cqmms(e):
                for j in range(16):
                    e.matmul(pcq[:, 0:512], a1T[:, j * 128:(j + 1) * 128],
                             wcq_sb[:, j * 768: j * 768 + 512], start=(j == 0), stop=(j == 15))
                    ins = e.matmul(pcq[:, 512:768], a1T[:, j * 128:(j + 1) * 128],
                                   wcq_sb[:, j * 768 + 512: (j + 1) * 768], start=(j == 0), stop=(j == 15))
                return ins

            def ckvmms(e):
                for j in range(16):
                    ins = e.matmul(pkv[:, 0:256], a1T[:, j * 128:(j + 1) * 128],
                                   wckv_sb[:, j * 256:(j + 1) * 256], start=(j == 0), stop=(j == 15))
                return ins

            def krmms(e):
                for j in range(16):
                    ins = e.matmul(pkr[0:32, 0:128], wkr_sb[:, j * 32:(j + 1) * 32],
                                   a1T[:, j * 128:(j + 1) * 128], start=(j == 0), stop=(j == 15))
                return ins
            w = [("ste", ma1), ("sw0", mwcast), ("sv0", mwprep)]
            if n > 0:
                w += [("ssi", nq_apply[n - 1]), ("ssi", nkv_apply[n - 1]), ("sv0", kr_done[n - 1])]
            A("tensor", w, cqmms, [("st0", 1)])
            mcq = M("st0")
            A("tensor", [], ckvmms, [("st0", 1)])
            mkv = M("st0")
            A("tensor", [], krmms, [("st0", 1)])
            mkr2 = M("st0")
            cq_done.append(M("st0"))
            wsqq = [("st0", mcq)]
            if n > 0:
                wsqq.append(("sv0", rms_red[n - 1]))
            A("scalar", wsqq,
              lambda e: e.activation(sqr_sb[:, 0:768], pcq[:, 0:768], AF.Square), [("ssi", 1)])
            A("vector", [("ssi", M("ssi"))],
              lambda e: e.reduce_sum(rstat[:, 0:1], sqr_sb[:, 0:768], axis=AX.X), [("sv0", 1)])
            A("scalar", [("sv0", M("sv0"))],
              lambda e: e.activation(rstat[:, 1:2], rstat[:, 0:1], AF.Sqrt,
                                     bias=cst[:, 1:2], scale=1.0 / 768.0), [("ssi", 1)])
            A("vector", [("ssi", M("ssi"))],
              lambda e: e.reciprocal(rstat[:, 2:3], rstat[:, 1:2]), [("sv0", 1)])
            w = [("sv0", M("sv0"))]
            if n > 0:
                w.append(("str", trq_done[n - 1]))
            A("scalar", w,
              lambda e: e.activation(nq_sb[:], pcq[:, 0:768], AF.Identity, scale=rstat[:, 2:3]),
              [("ssi", 1)])
            nq_apply.append(M("ssi"))
            wsqk = [("st0", mkv)]
            if n > 0:
                wsqk.append(("sv0", rms_red[n - 1]))
            A("scalar", wsqk,
              lambda e: e.activation(sqr_sb[:, 768:1024], pkv[:, 0:256], AF.Square), [("ssi", 1)])
            A("vector", [("ssi", M("ssi"))],
              lambda e: e.reduce_sum(rstat[:, 3:4], sqr_sb[:, 768:1024], axis=AX.X), [("sv0", 1)])
            rms_red.append(M("sv0"))
            A("scalar", [("sv0", M("sv0"))],
              lambda e: e.activation(rstat[:, 4:5], rstat[:, 3:4], AF.Sqrt,
                                     bias=cst[:, 1:2], scale=1.0 / 256.0), [("ssi", 1)])
            A("vector", [("ssi", M("ssi"))],
              lambda e: e.reciprocal(rstat[:, 5:6], rstat[:, 4:5]), [("sv0", 1)])
            w = [("sv0", M("sv0"))]
            if n > 0:
                w.append(("str", trq_done[n - 1]))
            A("scalar", w,
              lambda e: e.activation(nkv_sb[:], pkv[:, 0:256], AF.Identity, scale=rstat[:, 5:6]),
              [("ssi", 1)])
            nkv_apply.append(M("ssi"))
            A("vector", [("st0", mkr2), ("sw0", mwcast)],
              lambda e, n=n: e.tensor_mul(krT[0:32, n * 128:(n + 1) * 128], pkr[0:32, 0:128],
                                          cosk_sb[:, n * 128:(n + 1) * 128]),
              [("sv0", 1)])
            kr_done.append(M("sv0"))
            for j in range(6):
                w = [("ssi", nq_apply[n])]
                if j >= 2:
                    w.append(("ste", M("ste")))
                A("tensor", w,
                  lambda e, j=j: e.transpose(pnq[j % 2][:, 0:128], nq_sb[:, j * 128:(j + 1) * 128], idn_b[:]),
                  [("str", 1)])
                A("vector", [("str", M("str"))],
                  lambda e, j=j, n=n: e.tensor_copy(
                      nqT[:, j * 2048 + n * 128: j * 2048 + n * 128 + 128], pnq[j % 2][:, 0:128]),
                  [("ste", 1)])
            for j in range(2):
                A("tensor", [("ssi", nkv_apply[n]), ("ste", M("ste"))],
                  lambda e, j=j: e.transpose(pnq[j % 2][:, 0:128], nkv_sb[:, j * 128:(j + 1) * 128], idn_b[:]),
                  [("str", 1)])
                A("vector", [("str", M("str"))],
                  lambda e, j=j, n=n: e.tensor_copy(
                      nkvT[:, j * 2048 + n * 128: j * 2048 + n * 128 + 128], pnq[j % 2][:, 0:128]),
                  [("ste", 1)])
            trq_done.append(M("str"))
        mall = M("ste")
        mkrall = M("sv0")
        for j in range(2):
            A("sync", [("ste", mall)],
              lambda e, j=j: e.dma_start(nkv_in.rearrange("(j p) n -> j p n", p=128)[j],
                                         nkvT[:, j * 2048:(j + 1) * 2048]), [("swg", 16)])
        A("sync", [("sv0", mkrall)], lambda e: e.dma_start(krp_in, krT[0:32, :]), [("swg", 16)])
        mkvw = M("swg")
        A("gpsimd", [("swg", mkvw)],
          lambda e: e.collective_compute("AllGather", ALU.bypass, replica_groups=GRP4,
                                         ins=[nkv_in], outs=[nkv_all]), [("scc", 1)])
        A("gpsimd", [],
          lambda e: e.collective_compute("AllGather", ALU.bypass, replica_groups=GRP4,
                                         ins=[krp_in], outs=[krp_all]), [("scc", 1)])
        m_ag_kv = M("scc")
        for j in range(6):
            A("sync", [("ste", mall)],
              lambda e, j=j: e.dma_start(nq_in.rearrange("(j p) n -> j p n", p=128)[j],
                                         nqT[:, j * 2048:(j + 1) * 2048]), [("swg", 16)])
        mnqw = M("swg")
        A("gpsimd", [("swg", mnqw)],
          lambda e: e.collective_compute("AllGather", ALU.bypass, replica_groups=GRP4,
                                         ins=[nq_in], outs=[nq_all]), [("scc", 1)])
        m_ag_nq = M("scc")
        if debug:
            mAG = M("scc")
            A("gpsimd", [("scc", mAG)], lambda e: e.dma_start(nq_dbg, nq_all), [("so0", 16)])
            A("gpsimd", [], lambda e: e.dma_start(nkv_dbg, nkv_all), [("so0", 16)])
            A("gpsimd", [], lambda e: e.dma_start(krp_dbg, krp_all), [("so0", 16)])
            A("gpsimd", [("so0", M("so0"))], lambda e: e.nop())
        S_.replay(blk2)

    if n_phases <= 2:
        return nc

    with ExitStack() as _es:
        en_ = _es.enter_context
        kT0 = en_(nc.sbuf_tensor("kT0", [128, S], BF16))
        kT1 = en_(nc.sbuf_tensor("kT1", [128, S], BF16))
        kT2 = en_(nc.sbuf_tensor("kT2", [128, S], BF16))
        kT3 = en_(nc.sbuf_tensor("kT3", [128, S], BF16))
        qT0 = en_(nc.sbuf_tensor("qT0", [128, S], BF16))
        qT1 = en_(nc.sbuf_tensor("qT1", [128, S], BF16))
        qT2 = en_(nc.sbuf_tensor("qT2", [128, S], BF16))
        qT3 = en_(nc.sbuf_tensor("qT3", [128, S], BF16))
        V_all = en_(nc.sbuf_tensor("V_all", [128, 64 * 260], BF16))
        sq2 = en_(nc.sbuf_tensor("sq2", [128, 1024], BF16))
        gbuf = en_(nc.sbuf_tensor("gbuf", [128, 16], F32))
        gneg = en_(nc.sbuf_tensor("gneg", [128, 4], F32))
        snorm = en_(nc.sbuf_tensor("snorm", [128, 16], F32))
        kT = [kT0, kT1, kT2, kT3]
        qT = [qT0, qT1, qT2, qT3]

        with ExitStack() as _e2:
            en2 = _e2.enter_context
            nkvT_sb = en2(nc.sbuf_tensor("nkvT_sb", [128, 8 * 2048], BF16))
            ps0 = en2(nc.psum_tensor("ps0", [128, 1024], F32))
            ps1 = en2(nc.psum_tensor("ps1", [128, 1024], F32))
            po = en2(nc.psum_tensor("po", [128, 512], F32))
            blk3a = en2(nc.Block())
            psx = [ps0, ps1]

            A("sync", [("scc", m_ag_kv)], lambda e: e.dma_start(
                nkvT_sb[:].rearrange("p (r k n) -> p r k n", r=4, k=2),
                nkv_all.rearrange("(r k p) n -> p r k n", p=128, k=2)), [("sx", 16)])
            mnkvl = M("sx")

            kT_ready = []
            for h in range(HPC):
                A("vector", [], lambda e, h=h: e.memset(kT[h][:], 0.0), [("sv0", 1)])
                mz = M("sv0")
                skr = "skr0" if h % 2 == 0 else "skr1"
                for rr in range(4):
                    A("sync", [("sv0", mz), ("scc", m_ag_kv)],
                      lambda e, h=h, rr=rr: e.dma_start(
                          kT[h][32:64, :].rearrange("p (qc rn) -> p qc rn", qc=4)[:, :, rr * 512:(rr + 1) * 512],
                          krp_all.rearrange("(r d) (qc n) -> r d qc n", d=32, qc=4)[rr]),
                      [(skr, 16)])
                mkrl = M(skr)
                knev = []
                for kb in range(16):
                    def knmm(e, h=h, kb=kb):
                        for kt in range(4):
                            r, loc = kb % 4, (kb // 4) * 512 + kt * 128
                            for ktl in range(2):
                                ins = e.matmul(
                                    psx[kb % 2][64:128, kt * 128:kt * 128 + 128],
                                    wkn_sb[:, ktl * 256 + h * 64: ktl * 256 + h * 64 + 64],
                                    nkvT_sb[:, (r * 2 + ktl) * 2048 + loc: (r * 2 + ktl) * 2048 + loc + 128],
                                    start=(ktl == 0), stop=(ktl == 1))
                        return ins
                    w = [("sx", mnkvl), ("sv0", mz)]
                    if kb >= 2:
                        w.append(("ste", knev[kb - 2]))
                    elif h > 0:
                        w.append(("sv0", kT_ready[h - 1][1]))
                    A("tensor", w, knmm, [("st0", 1)])
                    mkn = M("st0")
                    A("vector", [("st0", mkn)],
                      lambda e, h=h, kb=kb: e.tensor_copy(kT[h][64:128, kb * 512:(kb + 1) * 512],
                                                          psx[kb % 2][64:128, 0:512]),
                      [("ste", 1)])
                    knev.append(M("ste"))
                mkdone = M("ste")
                sqm, gmm = [], []
                for kb in range(16):
                    w = [("ste", mkdone), (skr, mkrl)]
                    if kb >= 2:
                        w.append(("st0", gmm[kb - 2]))
                    A("scalar", w,
                      lambda e, h=h, kb=kb: e.activation(sq2[:, (kb % 2) * 512:(kb % 2) * 512 + 512],
                                                         kT[h][:, kb * 512:(kb + 1) * 512], AF.Square),
                      [("ssi", 1)])
                    sqm.append(M("ssi"))
                    w = [("ssi", sqm[kb])]
                    if kb >= 2:
                        w.append(("sv0", M("sv0")))
                    A("tensor", w,
                      lambda e, kb=kb: e.matmul(psx[kb % 2][0:1, 0:512], ones_b[:, 0:1],
                                                sq2[:, (kb % 2) * 512:(kb % 2) * 512 + 512],
                                                start=True, stop=True),
                      [("st0", 1)])
                    gmm.append(M("st0"))
                    A("vector", [("st0", gmm[kb])],
                      lambda e, kb=kb: e.reduce_max(gbuf[0:1, kb:kb + 1], psx[kb % 2][0:1, 0:512],
                                                    axis=AX.X),
                      [("sv0", 1)])
                A("vector", [("sv0", M("sv0"))],
                  lambda e: e.reduce_max(gbuf[0:1, 0:1], gbuf[0:1, 0:16], axis=AX.X), [("sv0", 1)])
                A("scalar", [("sv0", M("sv0"))],
                  lambda e: e.activation(gbuf[0:1, 1:2], gbuf[0:1, 0:1], AF.Sqrt), [("ssi", 1)])
                A("vector", [("ssi", M("ssi"))],
                  lambda e, h=h: e.tensor_scalar_mul(gneg[0:1, h:h + 1], gbuf[0:1, 1:2], -1.0),
                  [("sv0", 1)])
                mgam = M("sv0")
                A("vector", [("ssi", sqm[15])],
                  lambda e, h=h: e.memset(kT[h][0:1, :], 1.0), [("sv0", 1)])
                kT_ready.append((M("sv0"), mgam))

            A("vector", [], lambda e: e.memset(V_all[:], 1.0), [("sv0", 1)])
            mvz = M("sv0")
            vev = []
            for kt64 in range(64):
                r, loc = (kt64 // 4) % 4, (kt64 // 16) * 512 + (kt64 % 4) * 128

                def vmm(e, r=r, loc=loc):
                    for ktl in range(2):
                        ins = e.matmul(po[:, 0:256],
                                       nkvT_sb[:, (r * 2 + ktl) * 2048 + loc: (r * 2 + ktl) * 2048 + loc + 128],
                                       wv_sb[:, ktl * 256:(ktl + 1) * 256], start=(ktl == 0), stop=(ktl == 1))
                    return ins
                w = [("sx", mnkvl), ("sv0", mvz)]
                if kt64 >= 1:
                    w.append(("sv0", vev[kt64 - 1]))
                A("tensor", w, vmm, [("st0", 1)])
                A("vector", [("st0", M("st0"))],
                  lambda e, kt64=kt64: e.tensor_copy(
                      V_all[:, kt64 * 260: kt64 * 260 + 260].rearrange("p (h c) -> p h c", h=4)[:, :, 0:64],
                      po[:, 0:256].rearrange("p (h c) -> p h c", h=4)),
                  [("sv0", 1)])
                vev.append(M("sv0"))
            mvdone = M("sv0")
            for h in range(HPC):
                A("vector", [], lambda e, h=h: e.memset(qT[h][:], 0.0), [("sv0", 1)])
            mqz = M("sv0")
            A("vector", [("sv0", mqz)], lambda e: e.engine_nop())
            S_.replay(blk3a)

        with ExitStack() as _e2:
            en2 = _e2.enter_context
            cosq_sb = en2(nc.sbuf_tensor("cosq_sb", [64, S], BF16))
            mask_sb = en2(nc.sbuf_tensor("mask_sb", [128, 4 * 512], BF16))
            nqs0 = en2(nc.sbuf_tensor("nqs0", [128, 6 * 512], BF16))
            nqs1 = nqs0
            pTb0 = en2(nc.sbuf_tensor("pTb0", [128, 1536], BF16))
            pTb1 = en2(nc.sbuf_tensor("pTb1", [128, 1536], BF16))
            o2s0 = en2(nc.sbuf_tensor("o2s0", [128, 512], BF16))
            o2s1 = en2(nc.sbuf_tensor("o2s1", [128, 512], BF16))
            rvrow = en2(nc.sbuf_tensor("rvrow", [128, 512], BF16))
            srow = en2(nc.sbuf_tensor("srow3", [128, 512], F32))
            rbc = srow
            ones64 = en2(nc.sbuf_tensor("ones64", [128, 64], BF16))
            ps0 = en2(nc.psum_tensor("ps0b", [128, 1536], F32))
            ps1 = en2(nc.psum_tensor("ps1b", [128, 1536], F32))
            po0 = en2(nc.psum_tensor("po0b", [128, 512], F32))
            po1 = en2(nc.psum_tensor("po1b", [128, 512], F32))
            blk3b = en2(nc.Block())
            nqs, pTb, o2s, psx = [nqs0, nqs1], [pTb0, pTb1], [o2s0, o2s1], [ps0, ps1]
            pob = [po0, po1]
            pm = po0

            A("gpsimd", [], lambda e: e.dma_start(cosq_sb[32:64, :], cosq), [("sw0", 16)])
            A("vector", [], lambda e: e.memset(ones64[:], 1.0), [("sv0", 1)])
            A("sync", [], lambda e: e.dma_start(mask_sb[:], masks), [("sx", 16)])
            mcosq = M("sw0")

            qn_ev = {}
            bm_m = []
            qb_done = []
            for qb in range(16):
                rr, col0 = qb % 4, (qb // 4) * 512
                w = [("scc", m_ag_nq)]
                if qb >= 1:
                    w.append(("st0", qb_done[qb - 1]))
                src = nq_all.rearrange("(r j p) n -> r p j n", j=6, p=128)[rr][:, :, col0:col0 + 512]
                A("sync", w, lambda e, qb=qb, src=src: e.dma_start(
                    nqs[qb % 2][:].rearrange("p (j n) -> p j n", j=6), src), [("sx", 16)])
                mnql = M("sx")
                for h in range(HPC):
                    i_lin = qb * HPC + h
                    def qnmm(e, h=h, qb=qb):
                        for j in range(6):
                            ins = e.matmul(psx[0][64:128, 0:512],
                                           wqn_sb[:, j * 256 + h * 64: j * 256 + h * 64 + 64],
                                           nqs[qb % 2][:, j * 512:(j + 1) * 512], start=(j == 0), stop=(j == 5))
                        return ins
                    w = [("sx", mnql)]
                    if (qb, h) != (0, 0):
                        w.append(("sv0", qn_ev[(qb, h - 1) if h > 0 else (qb - 1, HPC - 1)]))
                    A("tensor", w, qnmm, [("st0", 1)])
                    A("vector", [("st0", M("st0"))],
                      lambda e, h=h, qb=qb: e.tensor_copy(qT[h][64:128, qb * 512:(qb + 1) * 512],
                                                          psx[0][64:128, 0:512]),
                      [("sv0", 1)])

                    def qrmm(e, h=h, qb=qb):
                        for j in range(6):
                            ins = e.matmul(psx[1][32:64, 0:512],
                                           wqr_sb[:, j * 128 + h * 32: j * 128 + h * 32 + 32],
                                           nqs[qb % 2][:, j * 512:(j + 1) * 512], start=(j == 0), stop=(j == 5))
                        return ins
                    A("tensor", [], qrmm, [("st0", 1)])
                    A("vector", [("st0", M("st0")), ("sw0", mcosq)],
                      lambda e, h=h, qb=qb: e.tensor_mul(qT[h][32:64, qb * 512:(qb + 1) * 512],
                                                         psx[1][32:64, 0:512],
                                                         cosq_sb[32:64, qb * 512:(qb + 1) * 512]),
                      [("sv0", 1)])
                    mqre = M("sv0")
                    hh = (i_lin % 2) * 512
                    w = [("sv0", mqre)]
                    if i_lin >= 2:
                        w.append(("st0", bm_m[i_lin - 2]))
                    A("scalar", w,
                      lambda e, h=h, qb=qb, hh=hh: e.activation(sq2[:, hh:hh + 512],
                                                                qT[h][:, qb * 512:(qb + 1) * 512], AF.Square),
                      [("ssi", 1)])
                    A("tensor", [("ssi", M("ssi"))],
                      lambda e, hh=hh: e.matmul(pm[0:1, hh:hh + 512] if hh == 0 else pm[0:1, 0:512],
                                                ones_b[:, 0:1], sq2[:, hh:hh + 512], start=True, stop=True)
                      , [("st0", 1)])
                    bm_m.append(M("st0"))
                    wbr = [("st0", bm_m[-1])]
                    if i_lin >= 1:
                        wbr.append(("sv0", qn_ev[(qb, h - 1) if h >= 1 else (qb - 1, HPC - 1)]))
                    A("scalar", wbr,
                      lambda e: e.activation(srow[0:1, 0:512], pm[0:1, 0:512], AF.Sqrt),
                      [("ssi", 1)])
                    A("vector", [("ssi", M("ssi")), ("sv0", kT_ready[h][1])],
                      lambda e, h=h, qb=qb: e.tensor_scalar_mul(
                          qT[h][0:1, qb * 512:(qb + 1) * 512], srow[0:1, 0:512],
                          gneg[0:1, h:h + 1]),
                      [("sv0", 1)])
                    qn_ev[(qb, h)] = M("sv0")
                qb_done.append(M("st0"))
            mqT = M("sv0")

            exp_m = {}
            omm_m = {}
            norm_m = {}
            o_dma = {}
            bcc_m = {}
            last_exp = {}
            for qb in range(16):
                nkt = 4 * qb + 4
                gsizes = [3] * (nkt // 3) + ([nkt % 3] if nkt % 3 else [])
                ng = len(gsizes)
                kt0s = [sum(gsizes[:g]) for g in range(ng)]
                for h in range(HPC):
                    kw = kT_ready[h][0]
                    il = qb * HPC + h
                    po = pob[il % 2]
                    msc = {}

                    def sco(g):
                        gsz, kt0g = gsizes[g], kt0s[g]
                        for kk in range(gsz):
                            kt = kt0g + kk
                            w = []
                            if kk == 0:
                                if g == 0:
                                    w += [("sv0", mqT), ("sv0", kw)]
                                if g >= 2:
                                    w.append(("ssi", exp_m[(il, g - 2)]))
                                elif il > 0 and (il - 1, g % 2) in last_exp:
                                    w.append(("ssi", last_exp[(il - 1, g % 2)]))
                            A("tensor", w,
                              lambda e, h=h, qb=qb, kt=kt, g=g, kk=kk: e.matmul(
                                  psx[g % 2][:, kk * 512:(kk + 1) * 512],
                                  kT[h][:, kt * 128:(kt + 1) * 128],
                                  qT[h][:, qb * 512:(qb + 1) * 512], start=True, stop=True),
                              [("st0", 1)])
                        msc[g] = M("st0")

                    def expf(g):
                        gsz = gsizes[g]
                        w = [("st0", msc[g])]
                        if g >= 2:
                            w.append(("st0", omm_m[(il, g - 2)]))
                        elif g == 1 and ng > 1:
                            pass
                        if g < 2 and il > 0:
                            w.append(("st0", omm_m[(il - 1, "last")]))
                        A("scalar", w,
                          lambda e, g=g, gsz=gsz: e.activation(pTb[g % 2][:, 0:gsz * 512],
                                                               psx[g % 2][:, 0:gsz * 512], AF.Exp),
                          [("ssi", 1)])
                        exp_m[(il, g)] = M("ssi")
                        last_exp[(il, g % 2)] = exp_m[(il, g)]

                    def of(g):
                        gsz, kt0g = gsizes[g], kt0s[g]
                        mask_done = 0
                        for kk in range(gsz):
                            kt = kt0g + kk
                            d = kt - 4 * qb
                            if d >= 0:
                                A("vector", [("ssi", exp_m[(il, g)])],
                                  lambda e, g=g, kk=kk, d=d: e.tensor_mul(
                                      pTb[g % 2][:, kk * 512:(kk + 1) * 512],
                                      pTb[g % 2][:, kk * 512:(kk + 1) * 512],
                                      mask_sb[:, d * 512:(d + 1) * 512]),
                                  [("sv0", 1)])
                                mask_done = M("sv0")

                        def omms(e, h=h, g=g, gsz=gsz, kt0g=kt0s[g], nkt=nkt, po=po):
                            for kk in range(gsz):
                                kt = kt0g + kk
                                ins = e.matmul(po[0:65, 0:512],
                                               V_all[:, kt * 260 + h * 65: kt * 260 + h * 65 + 65],
                                               pTb[g % 2][:, kk * 512:(kk + 1) * 512],
                                               start=(kt == 0), stop=(kt == nkt - 1))
                            return ins
                        w = [("ssi", exp_m[(il, g)])]
                        if g == 0:
                            w.append(("sv0", mvdone))
                            if il >= 2:
                                w.append(("sv0", norm_m[il - 2]))
                            if il >= 1:
                                w.append(("sv0", bcc_m[il - 1]))
                        if mask_done:
                            w.append(("sv0", mask_done))
                        A("tensor", w, omms, [("st0", 1)])
                        omm_m[(il, g)] = M("st0")

                    sco(0)
                    expf(0)
                    for g in range(ng):
                        if g + 1 < ng:
                            sco(g + 1)
                            expf(g + 1)
                        of(g)
                    omm_m[(il, "last")] = omm_m[(il, ng - 1)]
                    mo = omm_m[(il, ng - 1)]
                    A("vector", [("st0", mo)],
                      lambda e, po=po: e.tensor_scalar_max(srow[64:65, 0:512], po[64:65, 0:512], 1e-30),
                      [("sv0", 1)])
                    A("scalar", [("sv0", M("sv0"))],
                      lambda e: e.activation(srow[64:65, 0:512], srow[64:65, 0:512], AF.Ln),
                      [("ssi", 1)])
                    A("scalar", [("ssi", M("ssi"))],
                      lambda e: e.activation(rvrow[64:65, 0:512], srow[64:65, 0:512], AF.Exp, scale=-1.0),
                      [("ssi", 1)])
                    mrv = M("ssi")
                    pbc = pob[(il + 1) % 2]
                    wb = [("ssi", mrv)]
                    if il >= 1:
                        wb.append(("sv0", norm_m[il - 1]))
                    A("tensor", wb,
                      lambda e, pbc=pbc: e.matmul(pbc[0:64, 0:512], ones64[64:65, 0:64],
                                                  rvrow[64:65, 0:512], start=True, stop=True),
                      [("st0", 1)])
                    mbc = M("st0")
                    A("vector", [("st0", mbc)],
                      lambda e, pbc=pbc: e.tensor_copy(srow[0:64, 0:512], pbc[0:64, 0:512]),
                      [("sv0", 1)])
                    bcc_m[il] = M("sv0")
                    w = [("sv0", bcc_m[il])]
                    if il >= 2:
                        w.append(("so0" if il % 2 == 0 else "so1", o_dma.get(il - 2, 0)))
                    A("vector", w,
                      lambda e, po=po, il=il: e.tensor_mul(o2s[il % 2][0:64, 0:512],
                                                           po[0:64, 0:512], srow[0:64, 0:512]),
                      [("sv0", 1)])
                    norm_m[il] = M("sv0")
                    dstv = o_in.rearrange("(h p) q -> p h q", p=64)[:, h, qb * 512:(qb + 1) * 512]
                    sso = "so0" if il % 2 == 0 else "so1"
                    A("gpsimd", [("sv0", norm_m[il])],
                      lambda e, il=il, dstv=dstv: e.dma_start(dstv, o2s[il % 2][0:64, 0:512]),
                      [(sso, 16)])
                    o_dma[il] = M(sso)
            A("gpsimd", [("so0", M("so0")), ("so1", M("so1"))], lambda e: e.nop())
            S_.replay(blk3b)

    if n_phases <= 3:
        return nc

    with ExitStack() as _es:
        en_ = _es.enter_context
        wpj_sb = en_(nc.sbuf_tensor("wpj_sb", [128, 2 * 2048], BF16))
        oT0 = en_(nc.sbuf_tensor("oT0", [128, 2 * 2048], BF16))
        oT1 = en_(nc.sbuf_tensor("oT1", [128, 2 * 2048], BF16))
        yp0 = en_(nc.sbuf_tensor("yp0", [128, 2048], BF16))
        yp1 = en_(nc.sbuf_tensor("yp1", [128, 2048], BF16))
        yo0 = en_(nc.sbuf_tensor("yo0", [128, 2048], BF16))
        yo1 = en_(nc.sbuf_tensor("yo1", [128, 2048], BF16))
        x1r0 = en_(nc.sbuf_tensor("x1r0", [128, 2048], F32))
        x1r1 = en_(nc.sbuf_tensor("x1r1", [128, 2048], F32))
        x2o0 = en_(nc.sbuf_tensor("x2o0", [128, 2048], F32))
        x2o1 = en_(nc.sbuf_tensor("x2o1", [128, 2048], F32))
        py0 = en_(nc.psum_tensor("py0", [128, 512], F32))
        py1 = en_(nc.psum_tensor("py1", [128, 512], F32))
        blk4 = en_(nc.Block())
        py = [py0, py1]
        oT = [oT0, oT1]
        yp = [yp0, yp1]

        A("gpsimd", [], lambda e: e.dma_start(
            wpj_sb[:].rearrange("p (u n) -> p u n", u=2),
            wproj.rearrange("(u p) n -> p u n", p=128)), [("sw0", 16)])
        mwp = M("sw0")
        chunk_done = {}
        yp_dma = {}
        mrs = {}
        for qc in range(4):
            sld = "sx0" if qc % 2 == 0 else "sx1"
            w = []
            if qc >= 2:
                w.append(("st0", chunk_done[qc - 2]))
            src = o_in.rearrange("(u p) q -> p u q", p=128)[:, :, 2048 * qc: 2048 * (qc + 1)]
            A("sync", w, lambda e, qc=qc, src=src: e.dma_start(
                oT[qc % 2][:].rearrange("p (u q) -> p u q", u=2), src), [(sld, 16)])
            mol = M(sld)
            for n in range(16):
                nq_ = qc * 16 + n
                for nb in range(4):
                    def pjmms(e, qc=qc, n=n, nb=nb):
                        for u in range(2):
                            ins = e.matmul(py[nb % 2][:], oT[qc % 2][:, u * 2048 + n * 128: u * 2048 + n * 128 + 128],
                                           wpj_sb[:, u * 2048 + nb * 512: u * 2048 + nb * 512 + 512],
                                           start=(u == 0), stop=(u == 1))
                        return ins
                    w = [(sld, mol), ("sw0", mwp)]
                    if nq_ * 4 + nb >= 2:
                        w.append(("sxo", M("sxo") - 1))
                    A("tensor", w, pjmms, [("st0", 1)])
                    mpy = M("st0")
                    w = [("st0", mpy)]
                    if nq_ >= 2:
                        w.append(("so0" if nq_ % 2 == 0 else "so1", yp_dma.get(nq_ - 2, 0)))
                    A("vector", w,
                      lambda e, nq_=nq_, nb=nb: e.tensor_copy(yp[nq_ % 2][:, nb * 512:(nb + 1) * 512],
                                                              py[nb % 2][:]),
                      [("sxo", 1)])
                mev = M("sxo")
                sso = "so0" if nq_ % 2 == 0 else "so1"
                dstv = yp_in_c[qc].rearrange("(n p) d -> n p d", p=128)[n]
                A("gpsimd", [("sxo", mev)],
                  lambda e, nq_=nq_, dstv=dstv: e.dma_start(dstv, yp[nq_ % 2][:]), [(sso, 16)])
                yp_dma[nq_] = M(sso)
            chunk_done[qc] = M("st0")
            A("gpsimd", [("so0", yp_dma[qc * 16 + 14]), ("so1", yp_dma[qc * 16 + 15])],
              lambda e, qc=qc: e.collective_compute("ReduceScatter", ALU.add, replica_groups=GRP4,
                                                    ins=[yp_in_c[qc]], outs=[yp_out_c[qc]]),
              [("scc", 1)])
            mrs[qc] = M("scc")
        yo, x1r, x2o = [yo0, yo1], [x1r0, x1r1], [x2o0, x2o1]
        res_m = {}
        res_out = {}
        for qc in range(4):
            for nn in range(4):
                n = qc * 4 + nn
                p2 = n % 2
                ssx = "sx0" if p2 == 0 else "sx1"
                ssr = "sxr0" if p2 == 0 else "sxr1"
                sso = "so0" if p2 == 0 else "so1"
                w = [("scc", mrs[qc])]
                if n >= 2:
                    w.append(("sxo", res_m[n - 2]))
                srcy = yp_out_c[qc].rearrange("(m p) d -> m p d", p=128)[nn]
                A("sync", w, lambda e, n=n, srcy=srcy: e.dma_start(yo[n % 2][:], srcy), [(ssx, 16)])
                my1 = M(ssx)
                srcx = x1d.rearrange("(n p) d -> n p d", p=128)[n]
                wx = [("sxo", res_m[n - 2])] if n >= 2 else []
                A("scalar", wx, lambda e, n=n, srcx=srcx: e.dma_start(x1r[n % 2][:], srcx), [(ssr, 16)])
                mx1 = M(ssr)
                w2 = [(ssx, my1), (ssr, mx1)]
                if n >= 2:
                    w2.append((sso, res_out[n - 2]))
                A("vector", w2,
                  lambda e, n=n: e.tensor_add(x2o[n % 2][:], x1r[n % 2][:], yo[n % 2][:]), [("sxo", 1)])
                res_m[n] = M("sxo")
                dstv = x2d.rearrange("(n p) d -> n p d", p=128)[n]
                A("gpsimd", [("sxo", res_m[n])],
                  lambda e, n=n, dstv=dstv: e.dma_start(dstv, x2o[n % 2][:]), [(sso, 16)])
                res_out[n] = M(sso)
        A("gpsimd", [("so0", M("so0")), ("so1", M("so1"))], lambda e: e.nop())
        S_.replay(blk4)

    if n_phases <= 4:
        return nc

    ffn_pass(x2d, x2d, y_out, ("sy0", "sy1"), "f2")
    return nc


_CACHE = {}


def _get_nc():
    if "nc" not in _CACHE:
        _CACHE["nc"] = build()
    return _CACHE["nc"]


def make_consts():
    j = np.arange(16)
    inv = 10000.0 ** (-2.0 * j / 32.0)
    ang = np.arange(S, dtype=np.float64)[:, None] * inv[None, :]
    c = np.cos(ang)
    rows = np.repeat(c.T, 2, axis=0)
    cosq_np = (rows * SCL).astype(np.float32)
    cosk_full = (rows * SCL / 16.0).astype(np.float32)
    idnf_np = np.eye(128, dtype=np.float32)
    idnb_np = np.eye(128).astype(ml_dtypes.bfloat16)
    kk = np.arange(128)[:, None]
    qq = np.arange(512)[None, :]
    mlist = [(128 * d + kk <= qq).astype(ml_dtypes.bfloat16) for d in range(4)]
    masks_np = np.concatenate(mlist, axis=1)
    return cosq_np, cosk_full, idnf_np, idnb_np, masks_np


def make_in_maps(inputs):
    g = lambda k: np.asarray(inputs[k])
    x = np.ascontiguousarray(g("L0_attn_x").astype(np.float32, copy=False).reshape(B * S, D))
    gate_w = g("L0_ffn_gate_w"); up_w = g("L0_ffn_up_w"); down_w = g("L0_ffn_down_w")
    gate_s = g("L0_ffn_gate_scale").astype(np.float32); up_s = g("L0_ffn_up_scale").astype(np.float32)
    down_s = g("L0_ffn_down_scale").astype(np.float32)
    cosq_np, cosk_full, idnf_np, idnb_np, masks_np = make_consts()
    shared = dict(
        wcq=np.ascontiguousarray(g("L0_attn_w_cq").astype(np.float32)),
        wckv=np.ascontiguousarray(g("L0_attn_w_ckv").astype(np.float32)),
        wkr=np.ascontiguousarray(g("L0_attn_w_k_rope").astype(np.float32)),
        wproj=np.ascontiguousarray(g("L0_attn_w_proj").astype(np.float32)),
        qgT=np.ascontiguousarray(g("L0_attn_q_g").astype(np.float32).reshape(6, 128).T),
        kvgT=np.ascontiguousarray(g("L0_attn_kv_g").astype(np.float32).reshape(2, 128).T),
        dscT=np.ascontiguousarray(down_s.reshape(16, 128).T),
        cosq=cosq_np, idnf=idnf_np, idnb=idnb_np, masks=masks_np,
    )
    wqn_f = g("L0_attn_w_q_nope").astype(np.float32)
    wqr_f = g("L0_attn_w_q_rope").astype(np.float32)
    wkn_f = g("L0_attn_w_k_nope").astype(np.float32)
    wv_f = g("L0_attn_w_v").astype(np.float32)
    in_maps = []
    for c in range(8):
        gidx, r = c // 4, c % 4
        im = dict(shared)
        im["x_in"] = np.ascontiguousarray(
            x[8192 * gidx: 8192 * (gidx + 1)].reshape(4, 4, 512, D)[:, r].reshape(2048, D))
        im["wg_raw"] = np.ascontiguousarray(gate_w[1024 * c: 1024 * (c + 1)])
        im["wu_raw"] = np.ascontiguousarray(up_w[1024 * c: 1024 * (c + 1)])
        im["wd_raw"] = np.ascontiguousarray(down_w[:, 1024 * c: 1024 * (c + 1)])
        im["gscT"] = np.ascontiguousarray(gate_s[1024 * c: 1024 * (c + 1)].reshape(8, 128).T)
        im["uscT"] = np.ascontiguousarray(up_s[1024 * c: 1024 * (c + 1)].reshape(8, 128).T)
        im["wqn"] = np.ascontiguousarray(wqn_f[:, 256 * r: 256 * (r + 1)])
        im["wqr"] = np.ascontiguousarray(wqr_f[:, 128 * r: 128 * (r + 1)])
        im["wkn"] = np.ascontiguousarray(wkn_f[:, 256 * r: 256 * (r + 1)])
        im["wv"] = np.ascontiguousarray(wv_f[:, 256 * r: 256 * (r + 1)])
        im["cosk"] = np.ascontiguousarray(
            cosk_full.reshape(32, 4, 4, 512)[:, :, r].reshape(32, 2048))
        in_maps.append(im)
    return in_maps


def kernel(**inputs):
    nc = _get_nc()
    in_maps = make_in_maps(inputs)
    res = run_bass_kernel_spmd(nc, in_maps, core_ids=list(range(8)))
    arr = np.stack([np.asarray(res.results[c]["y_out"]) for c in range(8)])
    y = arr.reshape(2, 4, 4, 512, D).transpose(0, 2, 1, 3, 4).reshape(B, S, D)
    return np.ascontiguousarray(y).astype(np.float32, copy=False)
